# version 01
# baseline (speedup 1.0000x reference)
import math
from contextlib import ExitStack

import numpy as np
import ml_dtypes

import concourse.bass as bass
import concourse.mybir as mybir
from concourse.bass_utils import run_bass_kernel_spmd

F32 = mybir.dt.float32
BF16 = mybir.dt.bfloat16
ALU = mybir.AluOpType
ACTF = mybir.ActivationFunctionType
AX = mybir.AxisListType

D = 2048
TB = 2048
TO = 1024
NCH = 16
NTC = 16
NTO = 8
H = 16
DH = 128
G = 2
NG = H // G
GD = G * DH
INTER = 8192
NIC = 64
RSQRT_DH = 1.0 / math.sqrt(DH)
RS_8192 = 1.0 / math.sqrt(8192.0)
RS_2048 = 1.0 / math.sqrt(2048.0)
W_MEAN_SCALE = 1.0 / float(D * D)
LADDER_W = [128.0, 64.0, 32.0, 16.0, 8.0, 4.0, 2.0, 1.0]
QT = 512
NQT = TO // QT

ENGINES = ("pe", "dv", "sc", "sy", "gp")
_EMAP = {"pe": "tensor", "dv": "vector", "sc": "scalar", "sy": "sync", "gp": "gpsimd"}


class Prog:

    def __init__(self, nc):
        self.nc = nc
        self.ops = []
        self.ctr = {}
        self.handles = {}
        self.last = {"dv": None, "sc": None}

    def sem(self, name):
        h = self.nc.semaphore(name).__enter__()
        self.handles[name] = h
        self.ctr[name] = 0
        return name

    def emit(self, eng, fn, waits=(), inc=None, inck=1):
        ws = [w for w in waits if w is not None]
        if eng in ("dv", "sc"):
            if self.last[eng] is not None:
                ws.append(self.last[eng])
            inc = eng
        if inc is not None:
            self.ctr[inc] += inck
        tok = (inc, self.ctr[inc]) if inc else None
        if eng in ("dv", "sc"):
            self.last[eng] = tok
        self.ops.append((eng, fn, ws, inc, inck))
        return tok

    def dma(self, q, out, in_, waits=(), inc=None, **kw):
        assert inc is not None, "dma needs an explicit stream semaphore"
        return self.emit(
            q, lambda e, o=out, i=in_, kw=kw: e.dma_start(out=o, in_=i, **kw),
            waits=waits, inc=inc, inck=16)

    def pe(self, fn, waits=(), group_end=False):
        return self.emit("pe", fn, waits=waits, inc="pe" if group_end else None)

    def mmw(self):
        return [self.tok("dv"), self.tok("sc")]

    def wait_only(self, eng, waits):
        self.ops.append((eng, None, [w for w in waits if w is not None], None, None))

    def barrier(self):
        dma_sems = [s for s in self.ctr if s not in ("pe", "dv", "sc")]
        self.wait_only("sy", [self.tok(s) for s in dma_sems])
        self.ops.append(("BARRIER", None, None, None, None))

    def tok(self, sem):
        return (sem, self.ctr[sem]) if self.ctr[sem] > 0 else None

    def realize(self):
        segs = [[]]
        for op in self.ops:
            if op[0] == "BARRIER":
                segs.append([])
            else:
                segs[-1].append(op)
        for seg in segs:
            if not seg:
                continue
            per = {k: [] for k in ENGINES}
            for op in seg:
                per[op[0]].append(op)
            wm = {}
            with self.nc.Block() as blk:
                for ename in ENGINES:
                    if not per[ename]:
                        continue

                    def body(handle, _ops=tuple(per[ename]), _en=ename):
                        for (_e, fn, ws, inc, inck) in _ops:
                            wmax = {}
                            for (s, v) in ws:
                                if wmax.get(s, -1) < v:
                                    wmax[s] = v
                            for (s, v) in wmax.items():
                                if wm.get((_en, s), -1) >= v:
                                    continue
                                handle.wait_ge(self.handles[s], v)
                                wm[(_en, s)] = v
                            if fn is None:
                                continue
                            ins = fn(handle)
                            if inc is not None:
                                ins.then_inc(self.handles[inc], inck)
                    getattr(blk, _EMAP[ename])(body)


def build(dbg=None):
    nc = bass.Bass("TRN2", target_bir_lowering=False)

    x_in = nc.dram_tensor("x", [TB, D], F32, kind="ExternalInput").ap()
    wqkv = nc.dram_tensor("wqkv", [3 * NG * NCH * 128, GD], F32,
                          kind="ExternalInput").ap()
    woT = nc.dram_tensor("woT", [D, D], F32, kind="ExternalInput").ap()
    wug = nc.dram_tensor("wug", [2 * NIC * 128, NCH * 128], F32,
                         kind="ExternalInput").ap()
    wdT = nc.dram_tensor("wdT", [INTER, D], F32, kind="ExternalInput").ap()
    suug_in = nc.dram_tensor("suug", [128, NCH], F32, kind="ExternalInput").ap()
    sudp_in = nc.dram_tensor("sudp", [128, NIC], F32, kind="ExternalInput").ap()
    wscu_in = nc.dram_tensor("wscu", [128, NIC], F32, kind="ExternalInput").ap()
    wscg_in = nc.dram_tensor("wscg", [128, NIC], F32, kind="ExternalInput").ap()
    wsdp_in = nc.dram_tensor("wsdp", [1, D], F32, kind="ExternalInput").ap()
    h128_in = nc.dram_tensor("h128", [128, 128], BF16, kind="ExternalInput").ap()
    idf_in = nc.dram_tensor("idf", [128, 128], F32, kind="ExternalInput").ap()
    idb_in = nc.dram_tensor("idb", [128, 128], BF16, kind="ExternalInput").ap()
    out_hbm = nc.dram_tensor("out", [TO, D], F32, kind="ExternalOutput").ap()

    xnfm_sp = nc.dram_tensor("xnfm_sp", [D, TO], F32, kind="Internal").ap()
    att_sp = nc.dram_tensor("att_sp", [TO, D], F32, kind="Internal").ap()
    o_sp = nc.dram_tensor("o_sp", [TO, D], F32, kind="Internal").ap()
    z_sp = nc.dram_tensor("z_sp", [INTER, TO], F32, kind="Internal").ap()
    row_sp = nc.dram_tensor("row_sp", [8, TB], F32, kind="Internal").ap()

    dbg_outs = {}

    def dbgt(nm, shp, dt=F32):
        dbg_outs[nm] = nc.dram_tensor(nm, shp, dt, kind="ExternalOutput").ap()
        return dbg_outs[nm]

    ctx = ExitStack()
    with ctx:
        e = ctx.enter_context
        P = Prog(nc)
        for s in ("pe", "dv", "sc", "ld0", "wmA", "wmB", "x0", "x1", "row",
                  "bc", "w0", "w1", "at", "xn", "ug0", "ug1", "zq", "wd0",
                  "wd1", "xr", "or_", "st", "gxn", "gatt", "go", "gz"):
            P.sem(s)
        SY, GP = "sy", "gp"

        def fin(extra=()):
            P.wait_only("sy", [P.tok("st"), P.tok("gxn"), P.tok("gatt"),
                               P.tok("go"), P.tok("gz")] + list(extra))
            P.realize()

        h128 = e(nc.sbuf_tensor("h128s", [128, 128], BF16))
        idf = e(nc.sbuf_tensor("idfs", [128, 128], F32))
        idb = e(nc.sbuf_tensor("idbs", [128, 128], BF16))
        ones_f = e(nc.sbuf_tensor("ones_f", [1, 128], F32))
        ones_b = e(nc.sbuf_tensor("ones_b", [128, 1], BF16))
        epsc = e(nc.sbuf_tensor("epsc", [128, 1], F32))
        ainv = e(nc.sbuf_tensor("ainv", [128, NTC], F32))
        srow = e(nc.sbuf_tensor("srow", [128, NTC], F32))
        vscv = e(nc.sbuf_tensor("vscv", [128, NTC], F32))
        mqkT = e(nc.sbuf_tensor("mqkT", [128, 1], F32))
        mvT = e(nc.sbuf_tensor("mvT", [128, 1], F32))
        moT = e(nc.sbuf_tensor("moT", [128, 1], F32))
        siq = e(nc.sbuf_tensor("siq", [128, 1], F32))
        sik = e(nc.sbuf_tensor("sik", [128, 1], F32))
        siv = e(nc.sbuf_tensor("siv", [128, 1], F32))
        sio = e(nc.sbuf_tensor("sio", [128, 1], F32))
        crep_bk = e(nc.sbuf_tensor("crep_bk", [128, TB], F32))
        crep_aq = e(nc.sbuf_tensor("crep_aq", [128, TO], F32))
        crep_wdp = e(nc.sbuf_tensor("crep_wdp", [128, D], F32))
        suug = e(nc.sbuf_tensor("suugS", [128, NCH], F32))
        wsuF = e(nc.sbuf_tensor("wsuF", [128, NIC], F32))
        wsgF = e(nc.sbuf_tensor("wsgF", [128, NIC], F32))
        rowa = e(nc.sbuf_tensor("rowa", [16, 128], F32))
        rowb = e(nc.sbuf_tensor("rowb", [1, 512], F32))
        pb = [e(nc.psum_tensor(f"pb{i}", [128, 512], F32)) for i in range(7)]
        pbt = e(nc.psum_tensor("pbt", [128, 1024], BF16))

        t_h = P.dma(SY, h128[:], h128_in, inc="ld0")
        t_idf = P.dma(SY, idf[:], idf_in, inc="ld0")
        t_idb = P.dma(SY, idb[:], idb_in, inc="ld0")
        t_wdp = P.dma(SY, crep_wdp[:], wsdp_in.partition_broadcast(128), inc="ld0")
        t_su = P.dma(SY, suug[:], suug_in, inc="ld0")
        t_wsu0 = P.dma(SY, wsuF[:], wscu_in, inc="ld0")
        t_wsg0 = P.dma(SY, wsgF[:], wscg_in, inc="ld0")
        sudS = e(nc.sbuf_tensor("sudS", [128, NIC], F32))
        t_sud = P.dma(SY, sudS[:], sudp_in, inc="ld0")
        T0 = P.tok("ld0")
        t_h = t_idf = t_idb = t_wdp = t_su = t_wsu0 = t_wsg0 = t_sud = T0
        P.emit("dv", lambda v: v.memset(ones_f[:], 1.0))
        t_ob = P.emit("dv", lambda v: v.memset(ones_b[:], 1.0))
        t_eps = P.emit("dv", lambda v: v.memset(epsc[:], 1e-5))
        P.emit("dv", lambda v: v.tensor_tensor(wsuF[:], wsuF[:], sudS[:], ALU.mult),
               waits=[t_wsu0, t_sud])
        P.emit("dv", lambda v: v.tensor_single_scalar(wsuF[:], wsuF[:], RS_2048,
                                                      ALU.mult))
        t_wsu = P.emit("dv", lambda v: v.tensor_single_scalar(wsgF[:], wsgF[:],
                                                              RS_2048, ALU.mult),
                       waits=[t_wsg0])
        t_wsg = t_wsu

        with ExitStack() as c0:
            e0 = c0.enter_context
            wmb = e0(nc.sbuf_tensor("wmb", [128, 2 * D], F32))
            wabs = e0(nc.sbuf_tensor("wabs", [128, D], F32))
            wpart = e0(nc.sbuf_tensor("wpart", [128, 2], F32))
            wacc = e0(nc.sbuf_tensor("wacc", [128, 4], F32))
            w4 = e0(nc.sbuf_tensor("w4", [4, 128], F32))
            w4c = e0(nc.sbuf_tensor("w4c", [4, 1], F32))
            wsc1 = e0(nc.sbuf_tensor("wsc1", [1, 8], F32))

            for mi in range(4):
                P.emit("dv", lambda v, mi=mi: v.memset(wacc[:, mi:mi + 1], 0.0))
            srcs = []
            for m in range(3):
                for b4 in range(NG * NCH // 4):
                    r0 = (m * NG * NCH + b4 * 4) * 128
                    ap3 = bass.AP(wqkv.tensor, r0 * GD,
                                  [[GD, 128], [128 * GD, 4], [1, GD]])
                    srcs.append((m, ap3, 4 * GD))
            for fc in range(NCH):
                srcs.append((3, woT[fc * 128:(fc + 1) * 128, :], D))

            tld = [None, None]
            tab = [None, None]
            tadd = [None, None]
            for i, (mi, src, width) in enumerate(srcs):
                par = i % 2
                tld[par] = P.dma(SY, wmb[:, par * D:par * D + width], src,
                                 waits=[tab[par]],
                                 inc=("wmA" if par == 0 else "wmB"))
                tab[par] = P.emit(
                    "sc", lambda s, p=par, wd=width:
                    s.activation(wabs[:, 0:wd], wmb[:, p * D:p * D + wd],
                                 ACTF.Abs, bias=0.0, scale=1.0,
                                 accum_out=wpart[:, p:p + 1]),
                    waits=[tld[par], tadd[par]])
                tadd[par] = P.emit(
                    "dv", lambda v, p=par, mi=mi:
                    v.tensor_add(wacc[:, mi:mi + 1], wacc[:, mi:mi + 1],
                                 wpart[:, p:p + 1]),
                    waits=[tab[par]])
            t_tr = P.pe(lambda t: t.transpose(pb[0][0:4, 0:128], wacc[:], idf[:]),
                        waits=P.mmw() + [t_idf], group_end=True)
            P.emit("dv", lambda v: v.tensor_copy(w4[:], pb[0][0:4, 0:128]),
                   waits=[t_tr])
            t_rs = P.emit("dv", lambda v: v.reduce_sum(w4c[:], w4[:], AX.X))
            t_tr2 = P.pe(lambda t: t.transpose(pb[1][0:1, 0:4], w4c[:], idf[0:4, 0:4]),
                         waits=P.mmw(), group_end=True)
            P.emit("dv", lambda v: v.tensor_scalar(wsc1[0:1, 0:4], pb[1][0:1, 0:4],
                                                   W_MEAN_SCALE, 1e-5,
                                                   ALU.mult, ALU.max),
                   waits=[t_tr2])
            P.emit("dv", lambda v: v.reciprocal(wsc1[0:1, 4:8], wsc1[0:1, 0:4]))
            P.emit("dv", lambda v: v.tensor_tensor(wsc1[0:1, 0:1], wsc1[0:1, 0:1],
                                                   wsc1[0:1, 1:2], ALU.mult))
            t_p2 = P.emit("dv", lambda v: v.tensor_single_scalar(
                wsc1[0:1, 0:1], wsc1[0:1, 0:1], RSQRT_DH, ALU.mult))
            bb_src = [(0, mqkT), (2, mvT), (3, moT), (4, siq), (5, sik),
                      (6, siv), (7, sio)]
            t_bc = None
            for j, (ci, _) in enumerate(bb_src):
                t_bc = P.pe(lambda t, ci=ci, j=j:
                            t.matmul(pb[2][:, j:j + 1], ones_f[:],
                                     wsc1[0:1, ci:ci + 1], start=True, stop=True),
                            waits=P.mmw(), group_end=True)
            for j, (_, dstt) in enumerate(bb_src):
                P.emit("dv", lambda v, j=j, dt=dstt:
                       v.tensor_copy(dt[:], pb[2][:, j:j + 1]), waits=[t_bc])
            t_scal = P.tok("dv")

            if dbg == "ph0":
                P.dma(SY, dbgt("d_sc", [1, 8]), wsc1[0:1, 0:8], waits=[t_scal],
                      inc="st")
                fin()
                return nc, dbg_outs
            P.barrier()

        c13 = ctx.enter_context(ExitStack())
        xint = c13.enter_context(nc.sbuf_tensor("xint", [128, NCH * TB], BF16))
        with ExitStack() as c1:
            e1 = c1.enter_context
            xs = e1(nc.sbuf_tensor("xs", [128, 2 * D], F32))
            sqs = e1(nc.sbuf_tensor("sqs", [128, D], F32))
            xnt = e1(nc.sbuf_tensor("xnt", [128, D], F32))
            xib = e1(nc.sbuf_tensor("xib", [128, D], BF16))
            lb = e1(nc.sbuf_tensor("lb", [128, D], F32))
            lrem = e1(nc.sbuf_tensor("lrem", [128, D], F32))
            lcw = e1(nc.sbuf_tensor("lcw", [128, D], F32))
            st8 = e1(nc.sbuf_tensor("st8", [128, 8], F32))
            tevf = e1(nc.sbuf_tensor("tevf", [128, 128], F32))

            t_ld = [None, None]
            t_chunk_done = {}
            t_tevf_free = None
            for tc in range(NTC):
                par = tc % 2
                xt = xs[:, par * D:par * D + D]
                t_ld[par] = P.dma(SY, xt, x_in[tc * 128:(tc + 1) * 128, :],
                                  waits=[t_chunk_done.get(tc - 2)],
                                  inc=("x0" if par == 0 else "x1"))
                P.emit("dv", lambda v, xt=xt: v.reduce_sum(st8[:, 0:1], xt, AX.X),
                       waits=[t_ld[par]])
                P.emit("dv", lambda v: v.tensor_single_scalar(
                    st8[:, 1:2], st8[:, 0:1], 1.0 / D, ALU.mult))
                t3 = P.emit("dv", lambda v, xt=xt: v.tensor_scalar(
                    xnt[:], xt, st8[:, 1:2], None, ALU.subtract))
                t4 = P.emit("sc", lambda s: s.activation(
                    sqs[:], xnt[:], ACTF.Square, bias=0.0, scale=1.0,
                    accum_out=st8[:, 2:3]), waits=[t3])
                t5 = P.emit("sc", lambda s: s.activation(
                    st8[:, 3:4], st8[:, 2:3], ACTF.Sqrt, bias=epsc[:],
                    scale=1.0 / D), waits=[t_eps])
                P.emit("dv", lambda v: v.reciprocal(st8[:, 4:5], st8[:, 3:4]),
                       waits=[t5])
                t7 = P.emit("dv", lambda v: v.tensor_scalar(
                    xnt[:], xnt[:], st8[:, 4:5], None, ALU.mult))
                P.emit("dv", lambda v: v.reduce_max(
                    st8[:, 5:6], xnt[:], AX.X, apply_absolute_value=True))
                P.emit("dv", lambda v, tc=tc: v.tensor_scalar(
                    ainv[:, tc:tc + 1], st8[:, 5:6], 1e-5, 1.0 / 127.0,
                    ALU.max, ALU.mult))
                P.emit("dv", lambda v, tc=tc: v.reciprocal(
                    srow[:, tc:tc + 1], ainv[:, tc:tc + 1]))
                P.emit("dv", lambda v, tc=tc: v.tensor_scalar(
                    vscv[:, tc:tc + 1], ainv[:, tc:tc + 1], mvT[:], None, ALU.mult),
                    waits=[t_scal])
                P.emit("dv", lambda v, tc=tc: v.tensor_scalar(
                    lb[:], xnt[:], srow[:, tc:tc + 1], 128.5, ALU.mult, ALU.add))
                P.emit("dv", lambda v: v.tensor_copy(lrem[:], lb[:]))
                for w in LADDER_W:
                    P.emit("dv", lambda v, w=w: v.tensor_scalar(
                        lcw[:], lrem[:], w, w, ALU.is_ge, ALU.mult))
                    P.emit("dv", lambda v: v.tensor_sub(lrem[:], lrem[:], lcw[:]))
                tq = P.emit("dv", lambda v: v.scalar_tensor_tensor(
                    xib[:], lb[:], 128.0, lrem[:], ALU.subtract, ALU.subtract))
                for dc in range(NCH):
                    ttr = P.pe(lambda t, dc=dc: t.transpose(
                        pbt[:, 0:128], xib[:, dc * 128:(dc + 1) * 128], idb[:]),
                        waits=P.mmw() + [t_idb], group_end=True)
                    P.emit("dv", lambda v, dc=dc, tc=tc: v.tensor_copy(
                        xint[:, dc * TB + tc * 128: dc * TB + (tc + 1) * 128],
                        pbt[:, 0:128]), waits=[ttr])
                t_last = P.tok("dv")
                if tc < NTO:
                    for dc in range(NCH):
                        ttr = P.pe(lambda t, dc=dc: t.transpose(
                            pb[6][:, 0:128], xnt[:, dc * 128:(dc + 1) * 128],
                            idf[:]), waits=P.mmw(), group_end=True)
                        tev = P.emit("dv", lambda v: v.tensor_copy(
                            tevf[:], pb[6][:, 0:128]),
                            waits=[ttr, t_tevf_free])
                        tdm = P.dma(GP, xnfm_sp[dc * 128:(dc + 1) * 128,
                                                tc * 128:(tc + 1) * 128],
                                    tevf[:], waits=[tev], inc="gxn")
                        t_tevf_free = tdm
                    t_last = tdm
                t_chunk_done[tc] = t_last

            t_pr = P.pe(lambda t: t.transpose(pb[0][0:16, 0:128], ainv[:], idf[:]),
                        waits=P.mmw(), group_end=True)
            P.emit("dv", lambda v: v.tensor_copy(rowa[:], pb[0][0:16, 0:128]),
                   waits=[t_pr])
            t_pr3 = P.dma(SY, bass.AP(row_sp.tensor, 0, [[128, 16], [1, 128]]),
                          rowa[:], waits=[P.tok("dv")], inc="row")
            t_bk0 = P.dma(SY, crep_bk[:], row_sp[0:1, 0:TB].partition_broadcast(128),
                          waits=[t_pr3], inc="bc")
            t_bk = P.emit("dv", lambda v: v.tensor_scalar(
                crep_bk[:], crep_bk[:], mqkT[:], None, ALU.mult), waits=[t_bk0])
            t_aq = P.dma(SY, crep_aq[:], row_sp[0:1, 0:TO].partition_broadcast(128),
                         waits=[t_pr3, t_bk0], inc="bc")

            if dbg == "ph1":
                P.dma(SY, dbgt("d_xint", [128, NCH * TB], BF16), xint[:],
                      waits=[P.tok("dv")], inc="st")
                P.dma(SY, dbgt("d_ainv", [128, NTC]), ainv[:], waits=[P.tok("dv")],
                      inc="st")
                P.dma(SY, dbgt("d_srow", [128, NTC]), srow[:], waits=[P.tok("dv")],
                      inc="st")
                P.dma(SY, dbgt("d_crepbk", [128, TB]), crep_bk[:], waits=[t_bk],
                      inc="st")
                fin(extra=[t_aq])
                return nc, dbg_outs
            P.barrier()

        with ExitStack() as c2:
            e2 = c2.enter_context
            wtb = e2(nc.sbuf_tensor("wtb", [128, 2 * GD], F32))
            tq1 = e2(nc.sbuf_tensor("tq1", [128, GD], F32))
            tmat = e2(nc.sbuf_tensor("tmat", [128, NCH * GD], BF16))
            qh = e2(nc.sbuf_tensor("qh", [128, G * TO], BF16))
            ql = e2(nc.sbuf_tensor("ql", [128, G * TO], BF16))
            kh = e2(nc.sbuf_tensor("kh", [128, G * TB], BF16))
            kl = e2(nc.sbuf_tensor("kl", [128, G * TB], BF16))
            vh = e2(nc.sbuf_tensor("vh", [128, NTC * GD], BF16))
            vl = e2(nc.sbuf_tensor("vl", [128, NTC * GD], BF16))
            eh = e2(nc.sbuf_tensor("eh", [128, NTC * QT], BF16))
            el = e2(nc.sbuf_tensor("el", [128, NTC * QT], BF16))
            sS = e2(nc.sbuf_tensor("sS", [128, QT], F32))
            eS = e2(nc.sbuf_tensor("eS", [128, QT], F32))
            qsf = e2(nc.sbuf_tensor("qsf", [128, 512], F32))
            attf = e2(nc.sbuf_tensor("attf", [128, QT], F32))
            att_tok = e2(nc.sbuf_tensor("att_tok", [128, NTO * GD], F32))
            mh_t = e2(nc.sbuf_tensor("mh_t", [128, G * 8], F32))
            mst = e2(nc.sbuf_tensor("mst", [128, 4], F32))
            crep_mh = e2(nc.sbuf_tensor("crep_mh", [128, G * TO], F32))
            crep_ri = e2(nc.sbuf_tensor("crep_ri", [128, QT], F32))

            t_attspill = None
            t_wtb = [None, None]
            for g in range(NG):
                for m, sm in enumerate([siq[:], sik[:], siv[:]]):
                    t_pe_prev = P.tok("pe")
                    for dc in range(NCH):
                        par = dc % 2
                        r0 = ((m * NG + g) * NCH + dc) * 128
                        tld2 = P.dma(SY, wtb[:, par * GD:(par + 1) * GD],
                                     wqkv[r0:r0 + 128, :],
                                     waits=[t_wtb[par]],
                                     inc=("w0" if par == 0 else "w1"))
                        wsl = wtb[:, par * GD:(par + 1) * GD]
                        P.emit("dv", lambda v, wsl=wsl, sm=sm: v.tensor_scalar(
                            tq1[:], wsl, sm, 0.5, ALU.mult, ALU.is_ge),
                            waits=[tld2, t_pe_prev])
                        P.emit("dv", lambda v, wsl=wsl, sm=sm, dc=dc:
                               v.tensor_scalar(
                                   tmat[:, dc * GD:(dc + 1) * GD], wsl, sm,
                                   -0.5, ALU.mult, ALU.is_le))
                        t_wtb[par] = P.emit(
                            "dv", lambda v, dc=dc: v.tensor_sub(
                                tmat[:, dc * GD:(dc + 1) * GD], tq1[:],
                                tmat[:, dc * GD:(dc + 1) * GD]))
                    tern_done = P.tok("dv")

                    if dbg == "ph2" and g == 0 and m == 0:
                        P.dma(SY, dbgt("d_tmat", [128, NCH * GD], BF16), tmat[:],
                              waits=[tern_done], inc="st")
                        P.dma(SY, dbgt("d_siq", [128, 1]), siq[:],
                              waits=[tern_done], inc="st")
                        fin()
                        return nc, dbg_outs

                    if m == 0:
                        for dfc in range(G):
                            for nt in range(NQT):
                                bank = pb[(dfc * NQT + nt) % 4]
                                tm = None
                                for dc in range(NCH):
                                    st_ = (dc == 0)
                                    tm = P.pe(
                                        lambda t, dc=dc, dfc=dfc, nt=nt, bank=bank:
                                        t.matmul(
                                            bank[:, 0:512],
                                            tmat[:, dc * GD + dfc * 128:
                                                 dc * GD + (dfc + 1) * 128],
                                            xint[:, dc * TB + nt * 512:
                                                 dc * TB + (nt + 1) * 512],
                                            start=(dc == 0), stop=(dc == NCH - 1)),
                                        waits=(P.mmw() if st_ else []),
                                        group_end=(dc == NCH - 1))
                                o0 = dfc * TO + nt * 512
                                P.emit("dv", lambda v, bank=bank, nt=nt:
                                       v.tensor_tensor(
                                           qsf[:], bank[:, 0:512],
                                           crep_aq[:, nt * 512:(nt + 1) * 512],
                                           ALU.mult), waits=[tm, t_aq])
                                P.emit("dv", lambda v, o0=o0: v.tensor_copy(
                                    qh[:, o0:o0 + 512], qsf[:]))
                                P.emit("dv", lambda v, o0=o0: v.tensor_tensor(
                                    ql[:, o0:o0 + 512], qsf[:], qh[:, o0:o0 + 512],
                                    ALU.subtract))
                                if dbg == "ph2q2" and g == 0 and dfc == G - 1 and nt == NQT - 1:
                                    P.dma(SY, dbgt("d_qh2", [128, G * TO], BF16),
                                          qh[:], waits=[P.tok("dv")], inc="st")
                                    fin()
                                    return nc, dbg_outs
                                if dbg == "ph2q" and g == 0 and dfc == 0 and nt == 0:
                                    draw = dbgt("d_rawq", [128, 512])
                                    P.emit("dv", lambda v, bank=bank: v.tensor_copy(
                                        qsf[:], bank[:, 0:512]), waits=[tm])
                                    P.dma(SY, draw, qsf[:], waits=[P.tok("dv")],
                                          inc="st")
                                    fin()
                                    return nc, dbg_outs
                    elif m == 1:
                        for dfc in range(G):
                            for nt in range(4):
                                bank = pb[(dfc * 4 + nt) % 4]
                                tm = None
                                for dc in range(NCH):
                                    st_ = (dc == 0)
                                    tm = P.pe(
                                        lambda t, dc=dc, dfc=dfc, nt=nt, bank=bank:
                                        t.matmul(
                                            bank[:, 0:512],
                                            tmat[:, dc * GD + dfc * 128:
                                                 dc * GD + (dfc + 1) * 128],
                                            xint[:, dc * TB + nt * 512:
                                                 dc * TB + (nt + 1) * 512],
                                            start=(dc == 0), stop=(dc == NCH - 1)),
                                        waits=(P.mmw() if st_ else []),
                                        group_end=(dc == NCH - 1))
                                o0 = dfc * TB + nt * 512
                                P.emit("dv", lambda v, bank=bank, nt=nt:
                                       v.tensor_tensor(
                                           qsf[:], bank[:, 0:512],
                                           crep_bk[:, nt * 512:(nt + 1) * 512],
                                           ALU.mult), waits=[tm])
                                P.emit("dv", lambda v, o0=o0: v.tensor_copy(
                                    kh[:, o0:o0 + 512], qsf[:]))
                                P.emit("dv", lambda v, o0=o0: v.tensor_tensor(
                                    kl[:, o0:o0 + 512], qsf[:], kh[:, o0:o0 + 512],
                                    ALU.subtract))
                        if dbg == "ph2k" and g == 0:
                            P.dma(SY, dbgt("d_qh2", [128, G * TO], BF16),
                                  qh[:], waits=[P.tok("dv")], inc="st")
                            fin()
                            return nc, dbg_outs
                    else:
                        for tt in range(NTC):
                            bank = pb[tt % 4]
                            half = (tt % 2) * GD
                            tm = None
                            for dc in range(NCH):
                                st_ = (dc == 0)
                                tm = P.pe(
                                    lambda t, dc=dc, tt=tt, bank=bank:
                                    t.matmul(
                                        bank[:, 0:GD],
                                        xint[:, dc * TB + tt * 128:
                                             dc * TB + (tt + 1) * 128],
                                        tmat[:, dc * GD:(dc + 1) * GD],
                                        start=(dc == 0), stop=(dc == NCH - 1)),
                                    waits=(P.mmw() if st_ else []),
                                    group_end=(dc == NCH - 1))
                            P.emit("sc", lambda s, bank=bank, tt=tt:
                                   s.activation(
                                       vh[:, tt * GD:(tt + 1) * GD],
                                       bank[:, 0:GD], ACTF.Copy, bias=0.0,
                                       scale=vscv[:, tt:tt + 1]), waits=[tm])
                            P.emit("dv", lambda v, bank=bank, tt=tt:
                                   v.scalar_tensor_tensor(
                                       vl[:, tt * GD:(tt + 1) * GD],
                                       bank[:, 0:GD], vscv[:, tt:tt + 1],
                                       vh[:, tt * GD:(tt + 1) * GD],
                                       ALU.mult, ALU.subtract),
                                   waits=[P.tok("sc")])

                if dbg == "ph2v" and g == 0:
                    P.dma(SY, dbgt("d_qh2", [128, G * TO], BF16),
                          qh[:], waits=[P.tok("dv"), P.tok("sc")], inc="st")
                    fin()
                    return nc, dbg_outs
                for hl in range(G):
                    for qs in range(8):
                        tred = []
                        for ks in range(4):
                            tmm = P.pe(
                                lambda t, hl=hl, qs=qs, ks=ks:
                                t.matmul(pb[4 + (ks % 2)][:, 0:512],
                                         qh[:, hl * TO + qs * 128:
                                            hl * TO + (qs + 1) * 128],
                                         kh[:, hl * TB + ks * 512:
                                            hl * TB + (ks + 1) * 512],
                                         start=True, stop=True),
                                waits=P.mmw(), group_end=True)
                            tred.append(P.emit(
                                "dv", lambda v, ks=ks: v.reduce_max(
                                    mst[:, ks:ks + 1], pb[4 + (ks % 2)][:, 0:512],
                                    AX.X), waits=[tmm]))
                        P.emit("dv", lambda v, hl=hl, qs=qs: v.reduce_max(
                            mh_t[:, hl * 8 + qs:hl * 8 + qs + 1], mst[:, 0:4],
                            AX.X))
                t_crep_mh = []
                for hl in range(G):
                    ttr = P.pe(lambda t, hl=hl: t.transpose(
                        pb[6][0:8, 0:128], mh_t[:, hl * 8:(hl + 1) * 8], idf[:]),
                        waits=P.mmw(), group_end=True)
                    P.emit("dv", lambda v: v.tensor_copy(rowa[0:8, :],
                                                         pb[6][0:8, 0:128]),
                           waits=[ttr, P.tok("row")])
                    tdm = P.dma(SY, bass.AP(row_sp.tensor, 1 * TB,
                                            [[128, 8], [1, 128]]),
                                rowa[0:8, :], waits=[P.tok("dv"), P.tok("bc")],
                                inc="row")
                    tbc = P.dma(SY, crep_mh[:, hl * TO:(hl + 1) * TO],
                                row_sp[1:2, 0:TO].partition_broadcast(128),
                                waits=[tdm], inc="bc")
                    t_crep_mh.append(tbc)

                if dbg == "ph2a" and g == 0:
                    P.dma(SY, dbgt("d_qh2", [128, G * TO], BF16),
                          qh[:], waits=[P.tok("dv"), P.tok("sc")], inc="st")
                    fin()
                    return nc, dbg_outs
                for hl in range(G):
                    for qt in range(NQT):
                        for kc in range(NTC):
                            bank = pb[4 + kc % 2]
                            tm = None
                            for i3, (sw, mv) in enumerate([(kh, qh), (kh, ql),
                                                           (kl, qh)]):
                                tm = P.pe(
                                    lambda t, sw=sw, mv=mv, hl=hl, kc=kc, qt=qt,
                                    bank=bank, i3=i3:
                                    t.matmul(bank[:, 0:QT],
                                             sw[:, hl * TB + kc * 128:
                                                hl * TB + (kc + 1) * 128],
                                             mv[:, hl * TO + qt * QT:
                                                hl * TO + (qt + 1) * QT],
                                             start=(i3 == 0), stop=(i3 == 2)),
                                    waits=(P.mmw() + [t_crep_mh[hl]]
                                           if i3 == 0 else []),
                                    group_end=(i3 == 2))
                            P.emit("dv", lambda v, bank=bank, hl=hl, qt=qt:
                                   v.tensor_tensor(
                                       sS[:], bank[:, 0:QT],
                                       crep_mh[:, hl * TO + qt * QT:
                                               hl * TO + (qt + 1) * QT],
                                       ALU.subtract), waits=[tm])
                            texp = P.emit("sc", lambda s: s.activation(
                                eS[:], sS[:], ACTF.Exp, bias=0.0, scale=1.0),
                                waits=[P.tok("dv")])
                            P.emit("dv", lambda v, kc=kc: v.tensor_copy(
                                eh[:, kc * QT:(kc + 1) * QT], eS[:]),
                                waits=[texp])
                            P.emit("dv", lambda v, kc=kc: v.tensor_tensor(
                                el[:, kc * QT:(kc + 1) * QT], eS[:],
                                eh[:, kc * QT:(kc + 1) * QT], ALU.subtract))
                        tEdone = P.tok("dv")
                        tsum = None
                        first = True
                        for kc in range(NTC):
                            for ebuf in (eh, el):
                                tsum = P.pe(
                                    lambda t, kc=kc, ebuf=ebuf:
                                    t.matmul(pb[6][0:1, 0:QT], ones_b[:],
                                             ebuf[:, kc * QT:(kc + 1) * QT],
                                             start=(kc == 0 and ebuf is eh),
                                             stop=(kc == NTC - 1 and ebuf is el)),
                                    waits=(P.mmw() + [t_ob] if first else []),
                                    group_end=(kc == NTC - 1 and ebuf is el))
                                first = False
                        P.emit("dv", lambda v: v.reciprocal(
                            rowb[0:1, 0:QT], pb[6][0:1, 0:QT]),
                            waits=[tsum, P.tok("row")])
                        trd = P.dma(SY, bass.AP(row_sp.tensor, 2 * TB,
                                                [[1, 1], [1, QT]]),
                                    rowb[0:1, 0:QT],
                                    waits=[P.tok("dv"), P.tok("bc")], inc="row")
                        tbc = P.dma(SY, crep_ri[:],
                                    row_sp[2:3, 0:QT].partition_broadcast(128),
                                    waits=[trd], inc="bc")
                        tav = None
                        first = True
                        for kc in range(NTC):
                            for (sw, mv) in [(vh, eh), (vh, el), (vl, eh)]:
                                tav = P.pe(
                                    lambda t, sw=sw, mv=mv, hl=hl, kc=kc:
                                    t.matmul(pb[3][:, 0:QT],
                                             sw[:, kc * GD + hl * 128:
                                                kc * GD + (hl + 1) * 128],
                                             mv[:, kc * QT:(kc + 1) * QT],
                                             start=(kc == 0 and sw is vh and
                                                    mv is eh),
                                             stop=(kc == NTC - 1 and sw is vl)),
                                    waits=(P.mmw() if first else []),
                                    group_end=(kc == NTC - 1 and sw is vl))
                                first = False
                        tatt = P.emit("dv", lambda v: v.tensor_tensor(
                            attf[:], pb[3][:, 0:QT], crep_ri[:], ALU.mult),
                            waits=[tav, tbc])
                        for qs in range(QT // 128):
                            ttr = P.pe(lambda t, qs=qs: t.transpose(
                                pb[2][:, 0:128], attf[:, qs * 128:(qs + 1) * 128],
                                idf[:]), waits=P.mmw(), group_end=True)
                            qc = qt * (QT // 128) + qs
                            P.emit("dv", lambda v, qc=qc, hl=hl: v.tensor_copy(
                                att_tok[:, qc * GD + hl * 128:
                                        qc * GD + (hl + 1) * 128],
                                pb[2][:, 0:128]),
                                waits=[ttr, t_attspill])
                for qc in range(NTO):
                    t_attspill = P.dma(GP, att_sp[qc * 128:(qc + 1) * 128,
                                                  g * GD:(g + 1) * GD],
                                       att_tok[:, qc * GD:(qc + 1) * GD],
                                       waits=[P.tok("dv")], inc="gatt")

                if dbg == "ph3" and g == 0:
                    P.dma(SY, dbgt("d_att", [128, NTO * GD]), att_tok[:],
                          waits=[P.tok("dv")], inc="st")
                    P.dma(SY, dbgt("d_kh", [128, G * TB], BF16), kh[:],
                          waits=[P.tok("dv")], inc="st")
                    P.dma(SY, dbgt("d_qh", [128, G * TO], BF16), qh[:],
                          waits=[P.tok("dv")], inc="st")
                    P.dma(SY, dbgt("d_vh", [128, NTC * GD], BF16), vh[:],
                          waits=[P.tok("dv")], inc="st")
                    fin()
                    return nc, dbg_outs
            P.barrier()

        c13.close()
        with ExitStack() as c4:
            e4 = c4.enter_context
            to_m = e4(nc.sbuf_tensor("to_m", [128, NCH * D], BF16))
            afm = e4(nc.sbuf_tensor("afm", [128, NCH * TO], BF16))
            atr = e4(nc.sbuf_tensor("atr", [128, D], F32))
            aib = e4(nc.sbuf_tensor("aib", [128, D], BF16))
            lb4 = e4(nc.sbuf_tensor("lb4", [128, D], F32))
            lrem4 = e4(nc.sbuf_tensor("lrem4", [128, D], F32))
            lcw4 = e4(nc.sbuf_tensor("lcw4", [128, D], F32))
            wtb4 = e4(nc.sbuf_tensor("wtb4", [128, 2 * D], F32))
            tq4 = e4(nc.sbuf_tensor("tq4", [128, D], F32))
            osc = e4(nc.sbuf_tensor("osc", [128, NTO], F32))
            osb = e4(nc.sbuf_tensor("osb", [128, 512], F32))
            st4 = e4(nc.sbuf_tensor("st4", [128, 2], F32))

            t_tern4 = {}
            for fc in range(NCH):
                par = fc % 2
                tld4 = P.dma(SY, wtb4[:, par * D:(par + 1) * D],
                             woT[fc * 128:(fc + 1) * 128, :],
                             waits=[t_tern4.get(fc - 2)],
                             inc=("w0" if par == 0 else "w1"))
                P.emit("dv", lambda v, p=par: v.tensor_scalar(
                    tq4[:], wtb4[:, p * D:(p + 1) * D], sio[:], 0.5,
                    ALU.mult, ALU.is_ge), waits=[tld4])
                P.emit("dv", lambda v, p=par, fc=fc: v.tensor_scalar(
                    to_m[:, fc * D:(fc + 1) * D], wtb4[:, p * D:(p + 1) * D],
                    sio[:], -0.5, ALU.mult, ALU.is_le))
                t_tern4[fc] = P.emit("dv", lambda v, fc=fc: v.tensor_sub(
                    to_m[:, fc * D:(fc + 1) * D], tq4[:],
                    to_m[:, fc * D:(fc + 1) * D]))

            for qc in range(NTO):
                trl = P.dma(SY, atr[:], att_sp[qc * 128:(qc + 1) * 128, :],
                            waits=[t_attspill, P.tok("dv")], inc="at")
                P.emit("dv", lambda v: v.reduce_max(
                    st4[:, 0:1], atr[:], AX.X, apply_absolute_value=True),
                    waits=[trl])
                P.emit("dv", lambda v: v.tensor_scalar(
                    st4[:, 1:2], st4[:, 0:1], 1e-5, 1.0 / 127.0, ALU.max, ALU.mult))
                P.emit("dv", lambda v, qc=qc: v.tensor_scalar(
                    osc[:, qc:qc + 1], st4[:, 1:2], moT[:], None, ALU.mult))
                P.emit("dv", lambda v: v.reciprocal(st4[:, 0:1], st4[:, 1:2]))
                P.emit("dv", lambda v: v.tensor_scalar(
                    lb4[:], atr[:], st4[:, 0:1], 128.5, ALU.mult, ALU.add))
                P.emit("dv", lambda v: v.tensor_copy(lrem4[:], lb4[:]))
                for w in LADDER_W:
                    P.emit("dv", lambda v, w=w: v.tensor_scalar(
                        lcw4[:], lrem4[:], w, w, ALU.is_ge, ALU.mult))
                    P.emit("dv", lambda v: v.tensor_sub(lrem4[:], lrem4[:], lcw4[:]))
                P.emit("dv", lambda v: v.scalar_tensor_tensor(
                    aib[:], lb4[:], 128.0, lrem4[:], ALU.subtract, ALU.subtract))
                for fc in range(NCH):
                    ttr = P.pe(lambda t, fc=fc: t.transpose(
                        pbt[:, 0:128], aib[:, fc * 128:(fc + 1) * 128], idb[:]),
                        waits=P.mmw(), group_end=True)
                    P.emit("dv", lambda v, fc=fc, qc=qc: v.tensor_copy(
                        afm[:, fc * TO + qc * 128:fc * TO + (qc + 1) * 128],
                        pbt[:, 0:128]), waits=[ttr])

            t_osp = None
            for nd in range(4):
                for qt8 in range(NTO):
                    bank = pb[(nd * NTO + qt8) % 4]
                    tm = None
                    for fc in range(NCH):
                        st_ = (fc == 0)
                        tm = P.pe(lambda t, fc=fc, qt8=qt8, nd=nd, bank=bank:
                                  t.matmul(
                                      bank[:, 0:512],
                                      afm[:, fc * TO + qt8 * 128:
                                          fc * TO + (qt8 + 1) * 128],
                                      to_m[:, fc * D + nd * 512:
                                           fc * D + (nd + 1) * 512],
                                      start=(fc == 0), stop=(fc == NCH - 1)),
                                  waits=(P.mmw() if st_ else []),
                                  group_end=(fc == NCH - 1))
                    P.emit("dv", lambda v, bank=bank, qt8=qt8: v.tensor_scalar(
                        osb[:], bank[:, 0:512], osc[:, qt8:qt8 + 1], None,
                        ALU.mult), waits=[tm, t_osp])
                    t_osp = P.dma(GP, o_sp[qt8 * 128:(qt8 + 1) * 128,
                                           nd * 512:(nd + 1) * 512],
                                  osb[:], waits=[P.tok("dv")], inc="go")

            if dbg == "ph4":
                P.dma(SY, dbgt("d_osp", [TO, D]), o_sp, waits=[t_osp], inc="st")
                P.dma(SY, dbgt("d_afm", [128, NCH * TO], BF16), afm[:],
                      waits=[P.tok("dv")], inc="st")
                P.dma(SY, dbgt("d_osc", [128, NTO]), osc[:],
                      waits=[P.tok("dv")], inc="st")
                fin()
                return nc, dbg_outs
            P.barrier()

        with ExitStack() as c5:
            e5 = c5.enter_context
            xfh = e5(nc.sbuf_tensor("xfh", [128, NCH * TO], BF16))
            xfl = e5(nc.sbuf_tensor("xfl", [128, NCH * TO], BF16))
            x1h = e5(nc.sbuf_tensor("x1h", [128, TO], BF16))
            x1l = e5(nc.sbuf_tensor("x1l", [128, TO], BF16))
            sgb = e5(nc.sbuf_tensor("sgb", [128, 512], F32))
            ztb = e5(nc.sbuf_tensor("ztb", [128, TO], F32))

            c5a = ExitStack()
            xnb = c5a.enter_context(nc.sbuf_tensor("xnb", [128, NCH * TO], F32))
            t_xn = None
            for dc in range(NCH):
                t_xn = P.dma(SY, xnb[:, dc * TO:(dc + 1) * TO],
                             xnfm_sp[dc * 128:(dc + 1) * 128, :],
                             waits=[P.tok("gxn")], inc="xn")
            for dc in range(NCH):
                P.emit("dv", lambda v, dc=dc: v.tensor_scalar(
                    xnb[:, dc * TO:(dc + 1) * TO], xnb[:, dc * TO:(dc + 1) * TO],
                    suug[:, dc:dc + 1], None, ALU.mult),
                    waits=[t_xn, t_su])
            hstep = 1
            while hstep < NCH:
                for i in range(NCH):
                    if i & hstep:
                        continue
                    j = i + hstep
                    P.emit("dv", lambda v, i=i, j=j: v.tensor_add(
                        xnb[:, i * TO:(i + 1) * TO], xnb[:, i * TO:(i + 1) * TO],
                        xnb[:, j * TO:(j + 1) * TO]))
                    P.emit("dv", lambda v, i=i, j=j: v.scalar_tensor_tensor(
                        xnb[:, j * TO:(j + 1) * TO], xnb[:, j * TO:(j + 1) * TO],
                        -2.0, xnb[:, i * TO:(i + 1) * TO], ALU.mult, ALU.add))
                hstep *= 2
            for dc in range(NCH):
                th2 = None
                P.emit("dv", lambda v, dc=dc: v.tensor_copy(
                    x1h[:], xnb[:, dc * TO:(dc + 1) * TO]))
                th2 = P.emit("dv", lambda v, dc=dc: v.tensor_tensor(
                    x1l[:], xnb[:, dc * TO:(dc + 1) * TO], x1h[:], ALU.subtract))
                for nt in range(2):
                    bank = pb[4 + nt]
                    tm = None
                    for xop in (x1h, x1l):
                        st_ = (xop is x1h)
                        tm = P.pe(lambda t, xop=xop, nt=nt, bank=bank:
                                  t.matmul(bank[:, 0:512], h128[:],
                                           xop[:, nt * 512:(nt + 1) * 512],
                                           start=(xop is x1h), stop=(xop is x1l)),
                                  waits=(P.mmw() + [t_h] if st_ else []),
                                  group_end=(xop is x1l))
                    o0 = dc * TO + nt * 512
                    P.emit("sc", lambda s, bank=bank, o0=o0: s.activation(
                        xfh[:, o0:o0 + 512], bank[:, 0:512], ACTF.Copy,
                        bias=0.0, scale=1.0), waits=[tm])
                    P.emit("dv", lambda v, bank=bank, o0=o0: v.tensor_tensor(
                        xfl[:, o0:o0 + 512], bank[:, 0:512], xfh[:, o0:o0 + 512],
                        ALU.subtract), waits=[P.tok("sc")])

            c5a.close()
            wub = e5(nc.sbuf_tensor("wub", [128, 4 * D], F32))
            whb = e5(nc.sbuf_tensor("whb", [128, 4 * D], BF16))
            wlb = e5(nc.sbuf_tensor("wlb", [128, 4 * D], BF16))
            t_zsp = None
            t_wld = [P.tok("dv"), P.tok("dv")]
            for ic in range(NIC):
                par = ic % 2
                sem_ug = "ug0" if par == 0 else "ug1"
                w0_ = par * 2 * D
                tup = P.dma(SY, wub[:, w0_:w0_ + D], wug[ic * 128:(ic + 1) * 128, :],
                            waits=[t_wld[par]], inc=sem_ug)
                tgt = P.dma(SY, wub[:, w0_ + D:w0_ + 2 * D],
                            wug[(NIC + ic) * 128:(NIC + ic + 1) * 128, :],
                            waits=[t_wld[par]], inc=sem_ug)
                P.emit("dv", lambda v, w0_=w0_: v.tensor_copy(
                    whb[:, w0_:w0_ + 2 * D], wub[:, w0_:w0_ + 2 * D]),
                    waits=[tup, tgt])
                tcl = P.emit("dv", lambda v, w0_=w0_: v.tensor_tensor(
                    wlb[:, w0_:w0_ + 2 * D], wub[:, w0_:w0_ + 2 * D],
                    whb[:, w0_:w0_ + 2 * D], ALU.subtract))
                tz = {}
                for mi, off in [(0, 0), (1, D)]:
                    for nt in range(2):
                        bank = pb[mi * 2 + nt]
                        tm = None
                        first = True
                        for dc in range(NCH):
                            for (wsb, xsb) in [(whb, xfh), (whb, xfl), (wlb, xfh)]:
                                tm = P.pe(
                                    lambda t, wsb=wsb, xsb=xsb, dc=dc, nt=nt,
                                    off=off, bank=bank, par=par:
                                    t.matmul(
                                        bank[:, 0:512],
                                        wsb[:, par * 2 * D + off + dc * 128:
                                            par * 2 * D + off + (dc + 1) * 128],
                                        xsb[:, dc * TO + nt * 512:
                                            dc * TO + (nt + 1) * 512],
                                        start=(dc == 0 and wsb is whb and
                                               xsb is xfh),
                                        stop=(dc == NCH - 1 and wsb is wlb)),
                                    waits=(P.mmw() if first else []),
                                    group_end=(dc == NCH - 1 and wsb is wlb))
                                first = False
                        tz[(mi, nt)] = tm
                t_wld[par] = tz[(1, 1)]
                tzl = None
                for nt in range(2):
                    P.emit("sc", lambda s, nt=nt, ic=ic: s.activation(
                        sgb[:], pb[2 + nt][:, 0:512], ACTF.Silu, bias=0.0,
                        scale=wsgF[:, ic:ic + 1]),
                        waits=[tz[(1, nt)], t_wsg, P.tok("dv")])
                    tzl = P.emit("dv", lambda v, nt=nt, ic=ic:
                                 v.scalar_tensor_tensor(
                                     ztb[:, nt * 512:(nt + 1) * 512],
                                     pb[nt][:, 0:512], wsuF[:, ic:ic + 1],
                                     sgb[:], ALU.mult, ALU.mult),
                                 waits=[P.tok("sc"), tz[(0, nt)], t_wsu, t_zsp])
                t_zsp = P.dma(GP, z_sp[ic * 128:(ic + 1) * 128, :], ztb[:],
                              waits=[tzl], inc="gz")
            t_z_done = t_zsp

            if dbg == "ph5":
                P.dma(SY, dbgt("d_zsp", [INTER, TO]), z_sp, waits=[t_zsp], inc="st")
                P.dma(SY, dbgt("d_xfh", [128, NCH * TO], BF16), xfh[:],
                      waits=[P.tok("dv")], inc="st")
                fin()
                return nc, dbg_outs
            P.barrier()

        with ExitStack() as c6:
            e6 = c6.enter_context
            zq = e6(nc.sbuf_tensor("zq", [128, NIC * 256], F32))
            zfh = e6(nc.sbuf_tensor("zfh", [128, NIC * 256], BF16))
            zfl = e6(nc.sbuf_tensor("zfl", [128, NIC * 256], BF16))
            z1h = e6(nc.sbuf_tensor("z1h", [128, 256], BF16))
            z1l = e6(nc.sbuf_tensor("z1l", [128, 256], BF16))
            wdb = e6(nc.sbuf_tensor("wdb", [128, 2 * 1024], F32))
            wdh = e6(nc.sbuf_tensor("wdh", [128, 2 * 1024], BF16))
            wdl = e6(nc.sbuf_tensor("wdl", [128, 2 * 1024], BF16))
            fsb = e6(nc.sbuf_tensor("fsb", [128, 512], F32))
            xrb = e6(nc.sbuf_tensor("xrb", [128, D], F32))
            orb = e6(nc.sbuf_tensor("orb", [128, D], F32))
            outb = e6(nc.sbuf_tensor("outb", [128, 512], F32))

            t_q_done = None
            t_wl = [None, None]
            for qq in range(4):
                q0 = qq * 256
                t_zl4 = None
                for b4 in range(NIC // 4):
                    ap3 = bass.AP(z_sp.tensor, (b4 * 4 * 128) * TO + q0,
                                  [[TO, 128], [128 * TO, 4], [1, 256]])
                    t_zl4 = P.dma(SY, zq[:, b4 * 4 * 256:(b4 + 1) * 4 * 256], ap3,
                                  waits=[t_z_done, t_q_done], inc="zq")
                t_zl = {k: t_zl4 for k in range(NIC)}
                hstep = 1
                while hstep < NIC:
                    for i in range(NIC):
                        if i & hstep:
                            continue
                        j = i + hstep
                        P.emit("dv", lambda v, i=i, j=j: v.tensor_add(
                            zq[:, i * 256:(i + 1) * 256],
                            zq[:, i * 256:(i + 1) * 256],
                            zq[:, j * 256:(j + 1) * 256]),
                            waits=[t_zl[i], t_zl[j]])
                        P.emit("dv", lambda v, i=i, j=j: v.scalar_tensor_tensor(
                            zq[:, j * 256:(j + 1) * 256],
                            zq[:, j * 256:(j + 1) * 256],
                            -2.0, zq[:, i * 256:(i + 1) * 256],
                            ALU.mult, ALU.add))
                    hstep *= 2
                for ic in range(NIC):
                    P.emit("dv", lambda v, ic=ic: v.tensor_copy(
                        z1h[:], zq[:, ic * 256:(ic + 1) * 256]))
                    P.emit("dv", lambda v, ic=ic: v.tensor_tensor(
                        z1l[:], zq[:, ic * 256:(ic + 1) * 256], z1h[:],
                        ALU.subtract))
                    bank = pb[4 + ic % 2]
                    tm = None
                    for xop in (z1h, z1l):
                        st_ = (xop is z1h)
                        tm = P.pe(lambda t, xop=xop, bank=bank:
                                  t.matmul(bank[:, 0:256], h128[:], xop[:],
                                           start=(xop is z1h), stop=(xop is z1l)),
                                  waits=(P.mmw() + [t_h] if st_ else []),
                                  group_end=(xop is z1l))
                    o0 = ic * 256
                    P.emit("sc", lambda s, bank=bank, o0=o0: s.activation(
                        zfh[:, o0:o0 + 256], bank[:, 0:256], ACTF.Copy,
                        bias=0.0, scale=1.0), waits=[tm])
                    P.emit("dv", lambda v, bank=bank, o0=o0: v.tensor_tensor(
                        zfl[:, o0:o0 + 256], bank[:, 0:256], zfh[:, o0:o0 + 256],
                        ALU.subtract), waits=[P.tok("sc")])

                if dbg == "ph6a" and qq == 0:
                    P.dma(SY, dbgt("d_zfh", [128, NIC * 256], BF16), zfh[:],
                          waits=[P.tok("dv"), P.tok("sc")], inc="st")
                    P.dma(SY, dbgt("d_zq", [128, NIC * 256]), zq[:],
                          waits=[P.tok("dv")], inc="st")
                    fin()
                    return nc, dbg_outs
                t_ev_done = None
                for dh in range(2):
                    for ic in range(NIC):
                        par = ic % 2
                        tw = P.dma(SY, wdb[:, par * 1024:(par + 1) * 1024],
                                   wdT[ic * 128:(ic + 1) * 128,
                                       dh * 1024:(dh + 1) * 1024],
                                   waits=[t_wl[par]],
                                   inc=("wd0" if par == 0 else "wd1"))
                        P.emit("dv", lambda v, p=par: v.tensor_copy(
                            wdh[:, p * 1024:(p + 1) * 1024],
                            wdb[:, p * 1024:(p + 1) * 1024]), waits=[tw])
                        twl6 = P.emit("dv", lambda v, p=par: v.tensor_tensor(
                            wdl[:, p * 1024:(p + 1) * 1024],
                            wdb[:, p * 1024:(p + 1) * 1024],
                            wdh[:, p * 1024:(p + 1) * 1024], ALU.subtract))
                        tmm = None
                        for qs in range(2):
                            for nd in range(2):
                                bank = pb[qs * 2 + nd]
                                for (zsb, wsb) in [(zfh, wdh), (zfl, wdh),
                                                   (zfh, wdl)]:
                                    st_ = (ic == 0 and zsb is zfh and wsb is wdh)
                                    last_of_ic = (zsb is zfh and wsb is wdl and
                                                  qs == 1 and nd == 1)
                                    tmm = P.pe(
                                        lambda t, zsb=zsb, wsb=wsb, ic=ic, qs=qs,
                                        nd=nd, p=par, bank=bank:
                                        t.matmul(
                                            bank[:, 0:512],
                                            zsb[:, ic * 256 + qs * 128:
                                                ic * 256 + (qs + 1) * 128],
                                            wsb[:, p * 1024 + nd * 512:
                                                p * 1024 + (nd + 1) * 512],
                                            start=(ic == 0 and zsb is zfh and
                                                   wsb is wdh),
                                            stop=(ic == NIC - 1 and zsb is zfh and
                                                  wsb is wdl)),
                                        waits=(P.mmw() if st_ else
                                               ([twl6] if (zsb is zfh and
                                                wsb is wdh and qs == 0 and
                                                nd == 0) else [])),
                                        group_end=last_of_ic or
                                        (ic == NIC - 1 and zsb is zfh
                                         and wsb is wdl))
                        t_wl[par] = tmm
                    t_mm_done = P.tok("pe")
                    if dbg == "ph6d" and qq == 0 and dh == 0:
                        P.emit("dv", lambda v: v.tensor_copy(fsb[:], pb[0][:, 0:512]),
                               waits=[t_mm_done])
                        P.dma(SY, dbgt("d_dpsum", [128, 512]), fsb[:],
                              waits=[P.tok("dv")], inc="st")
                        P.emit("dv", lambda v: v.tensor_copy(fsb[:], pb[1][:, 0:512]),
                               waits=[t_mm_done])
                        P.dma(SY, dbgt("d_dpsum2", [128, 512]), fsb[:],
                              waits=[P.tok("dv")], inc="st")
                        P.dma(SY, dbgt("d_wdh", [128, 2048], BF16), wdh[:],
                              waits=[P.tok("dv")], inc="st")
                        P.dma(SY, dbgt("d_wdl", [128, 2048], BF16), wdl[:],
                              waits=[P.tok("dv")], inc="st")
                        P.dma(SY, dbgt("d_zfh6", [128, NIC * 256], BF16), zfh[:],
                              waits=[P.tok("dv")], inc="st")
                        P.dma(SY, dbgt("d_zfl6", [128, NIC * 256], BF16), zfl[:],
                              waits=[P.tok("dv")], inc="st")
                        fin()
                        return nc, dbg_outs
                    for qs in range(2):
                        r0 = q0 + qs * 128
                        txr = P.dma(SY, xrb[:], x_in[r0:r0 + 128, :],
                                    waits=[t_ev_done, P.tok("dv")], inc="xr")
                        tor = P.dma(SY, orb[:], o_sp[r0:r0 + 128, :],
                                    waits=[t_osp, t_ev_done, P.tok("dv")],
                                    inc="or_")
                        for nd in range(2):
                            bank = pb[qs * 2 + nd]
                            c0 = dh * 1024 + nd * 512
                            P.emit("dv", lambda v, bank=bank, c0=c0:
                                   v.scalar_tensor_tensor(
                                       fsb[:], bank[:, 0:512], RS_8192,
                                       crep_wdp[:, c0:c0 + 512],
                                       ALU.mult, ALU.mult),
                                   waits=[t_mm_done, t_wdp])
                            P.emit("dv", lambda v, c0=c0: v.tensor_tensor(
                                fsb[:], fsb[:], xrb[:, c0:c0 + 512], ALU.add),
                                waits=[txr])
                            P.emit("dv", lambda v, c0=c0: v.tensor_tensor(
                                outb[:], fsb[:], orb[:, c0:c0 + 512], ALU.add),
                                waits=[tor, P.tok("st")])
                            tst = P.dma(SY, out_hbm[r0:r0 + 128, c0:c0 + 512],
                                        outb[:], waits=[P.tok("dv")], inc="st")
                            t_ev_done = tst
                t_q_done = t_ev_done

        fin()
    return nc, dbg_outs



def _had128():
    i = np.arange(128)
    a = i[:, None] & i[None, :]
    pc = np.zeros((128, 128), dtype=np.int64)
    for b in range(7):
        pc += (a >> b) & 1
    return np.where(pc % 2 == 0, 1.0, -1.0)


def _prep_inputs(L0_attn_x, L0_attn_w_q, L0_attn_w_k, L0_attn_w_v, L0_attn_w_o,
                 L0_ffn_SU_ug, L0_ffn_SU_dp, L0_ffn_Wscale_ug, L0_ffn_Wscale_dp,
                 L0_ffn_W_up, L0_ffn_W_gate, L0_ffn_W_down):
    x = np.ascontiguousarray(np.asarray(L0_attn_x, dtype=np.float32))
    wq = np.asarray(L0_attn_w_q, dtype=np.float32)
    wk = np.asarray(L0_attn_w_k, dtype=np.float32)
    wv = np.asarray(L0_attn_w_v, dtype=np.float32)
    wo = np.asarray(L0_attn_w_o, dtype=np.float32)
    wup = np.asarray(L0_ffn_W_up, dtype=np.float32)
    wg = np.asarray(L0_ffn_W_gate, dtype=np.float32)
    wd = np.asarray(L0_ffn_W_down, dtype=np.float32)

    def tile_qkv(w):
        wT = w.T.reshape(NCH, 128, NG, GD)
        return wT.transpose(2, 0, 1, 3).reshape(NG * NCH * 128, GD)

    wqkv = np.ascontiguousarray(
        np.concatenate([tile_qkv(wq), tile_qkv(wk), tile_qkv(wv)], axis=0))
    woTh = np.ascontiguousarray(wo.T)

    def tile_ug(w):
        wT = w.T.reshape(NCH, 128, NIC, 128)
        return wT.transpose(2, 1, 0, 3).reshape(NIC * 128, NCH * 128)

    wugh = np.ascontiguousarray(np.concatenate([tile_ug(wup), tile_ug(wg)],
                                               axis=0))
    wdTh = np.ascontiguousarray(wd.T)

    sug = np.ascontiguousarray(
        np.asarray(L0_ffn_SU_ug, np.float32).reshape(NCH, 128).T)
    sdp = np.ascontiguousarray(
        np.asarray(L0_ffn_SU_dp, np.float32).reshape(NIC, 128).T)
    wsu = np.asarray(L0_ffn_Wscale_ug, np.float32)
    wscu = np.ascontiguousarray(wsu[:INTER].reshape(NIC, 128).T)
    wscg = np.ascontiguousarray(wsu[INTER:].reshape(NIC, 128).T)
    wsdp = np.ascontiguousarray(
        np.asarray(L0_ffn_Wscale_dp, np.float32).reshape(1, D))

    h128 = np.ascontiguousarray(_had128().astype(ml_dtypes.bfloat16))
    idf = np.eye(128, dtype=np.float32)
    idb = np.eye(128).astype(ml_dtypes.bfloat16)

    common = dict(wqkv=wqkv, woT=woTh, wug=wugh, wdT=wdTh, suug=sug, sudp=sdp,
                  wscu=wscu, wscg=wscg, wsdp=wsdp, h128=h128, idf=idf, idb=idb)
    in_maps = []
    for c in range(8):
        b, hh = c // 2, c % 2
        xb = x[b]
        xr = np.ascontiguousarray(
            np.concatenate([xb[hh * TO:(hh + 1) * TO],
                            xb[(1 - hh) * TO:(2 - hh) * TO]], axis=0))
        m = dict(common)
        m["x"] = xr
        in_maps.append(m)
    return in_maps


def kernel(**inputs):
    in_maps = _prep_inputs(**inputs)
    nc, _ = build()
    res = run_bass_kernel_spmd(nc, in_maps, core_ids=list(range(8)))
    out = np.empty((4, TB, D), dtype=np.float32)
    for c in range(8):
        b, hh = c // 2, c % 2
        out[b, hh * TO:(hh + 1) * TO] = res.results[c]["out"]
    return out
